# version 21
# baseline (speedup 1.0000x reference)
import math
import contextlib

import concourse.bass as bass
import concourse.mybir as mybir

F32 = mybir.dt.float32
BF16 = mybir.dt.bfloat16
I16 = mybir.dt.int16
AF = mybir.ActivationFunctionType
AL = mybir.AluOpType
AX = mybir.AxisListType.X

MAGIC = 12582912.0
TWO_PI = 6.283185307179586
PI = 3.141592653589793

NSEQ, T, D, H, HD, C = 2, 2048, 1024, 16, 64, 128
NT = T // C
NTILES = NSEQ * NT
FF = 3072
EPS = 1e-5
QMAX_A = 32767.0
QMAX_W = 7.0


class RecInst:
    __slots__ = ("thens",)

    def __init__(self):
        self.thens = []

    def then_inc(self, sem, k=1):
        self.thens.append((sem, k))
        return self


class RecEngine:

    def __init__(self):
        self.calls = []

    def __getattr__(self, name):
        def call(*a, **k):
            ri = RecInst()
            self.calls.append((name, a, k, ri))
            return ri
        return call

    def replay(self, eng):
        for name, a, k, ri in self.calls:
            inst = getattr(eng, name)(*a, **k)
            for sem, n in ri.thens:
                inst = inst.then_inc(sem, n)


class Sem:
    def __init__(self, es, nc, name):
        self.h = es.enter_context(nc.semaphore(name))
        self.n = 0

    def inc(self, inst, k=1):
        inst.then_inc(self.h, k)
        self.n += k
        return self.n

    def wait(self, eng, val=None):
        eng.wait_ge(self.h, self.n if val is None else val)


def build(phase="full", max_tiles=64, stages=99, dbg_on=False):
    nc = bass.Bass("TRN2", target_bir_lowering=False, debug=False,
                   detect_race_conditions=False)

    x_in = nc.dram_tensor("x", [NSEQ * T, D], F32, kind="ExternalInput").ap()
    Wq_h = nc.dram_tensor("Wq", [D, D], F32, kind="ExternalInput").ap()
    Wk_h = nc.dram_tensor("Wk", [D, D], F32, kind="ExternalInput").ap()
    Wv_h = nc.dram_tensor("Wv", [D, D], F32, kind="ExternalInput").ap()
    Wo_h = nc.dram_tensor("Wo", [D, D], F32, kind="ExternalInput").ap()
    Wg1_h = nc.dram_tensor("Wg1", [D, HD], F32, kind="ExternalInput").ap()
    Wg2_h = nc.dram_tensor("Wg2", [HD, D], F32, kind="ExternalInput").ap()
    gam_h = nc.dram_tensor("gamma", [1, D], F32, kind="ExternalInput").ap()
    bet_h = nc.dram_tensor("beta", [1, D], F32, kind="ExternalInput").ap()
    wgate_h = nc.dram_tensor("w_gate", [FF, D], F32, kind="ExternalInput").ap()
    wup_h = nc.dram_tensor("w_up", [FF, D], F32, kind="ExternalInput").ap()
    wdown_h = nc.dram_tensor("w_down", [D, FF], F32, kind="ExternalInput").ap()
    out_h = nc.dram_tensor("out", [NSEQ * T, D], F32, kind="ExternalOutput").ap()
    dbg_h = (nc.dram_tensor("dbg", [8 * 128, D], F32, kind="ExternalOutput").ap()
             if dbg_on else None)

    resid2_h = nc.dram_tensor("resid2_h", [NSEQ * T, D], F32).ap()
    ln2_h = nc.dram_tensor("ln2_h", [NSEQ * T, D], F32).ap()
    pmax_h = nc.dram_tensor("pmax_h", [4, 128], F32).ap()
    wupT_h = nc.dram_tensor("wupT_h", [D, FF], BF16).ap()
    wgateT_h = nc.dram_tensor("wgateT_h", [D, FF], BF16).ap()
    wdownT_h = nc.dram_tensor("wdownT_h", [FF, D], BF16).ap()
    o_h = nc.dram_tensor("o_h", [FF, NSEQ * T], BF16).ap()
    upraw_h = nc.dram_tensor("upraw_h", [FF, NSEQ * T], F32).ap()
    gateraw_h = nc.dram_tensor("gateraw_h", [FF, NSEQ * T], F32).ap()
    yqup_h = nc.dram_tensor("yqup_h", [FF, NSEQ * T], I16).ap()
    p_h = nc.dram_tensor("p_h", [FF, NSEQ * T], F32).ap()
    m2_h = nc.dram_tensor("m2_h", [FF, NSEQ * T], F32).ap()
    cc_in = [nc.dram_tensor(f"cc_in{i}", [1, 4], F32).ap() for i in range(4)]
    cc_out = [nc.dram_tensor(f"cc_out{i}", [1, 4], F32).ap() for i in range(4)]

    es = contextlib.ExitStack()
    with es:
        S = lambda n, shp, dt=F32: es.enter_context(nc.sbuf_tensor(n, shp, dt))

        mask16 = S("mask16", [C, C], BF16)
        cos16 = S("cos16", [128, NT * 32], BF16)
        sin16 = S("sin16", [128, NT * 32], BF16)
        gamma_r = S("gamma_r", [128, D])
        beta_r = S("beta_r", [128, D])
        wq16 = S("wq16", [128, 8 * D], BF16)
        wk16 = S("wk16", [128, 8 * D], BF16)
        wv16 = S("wv16", [128, 8 * D], BF16)
        wo16 = S("wo16", [128, 8 * D], BF16)
        wg1_16 = S("wg1_16", [128, 8 * HD], BF16)
        wg2_16 = S("wg2_16", [HD, D], BF16)
        scr = S("scr", [128, 1024])

        x_t = S("x_t", [128, 2 * D])
        xln16 = S("xln16", [128, 2 * D], BF16)
        xfm = S("xfm", [128, 8 * C * 2], BF16)
        qtm = S("qtm", [128, D], BF16)
        ktm = S("ktm", [128, D], BF16)
        qrot = S("qrot", [128, D], BF16)
        krot = S("krot", [128, D], BF16)
        vtm = S("vtm", [128, D], BF16)
        g1tm = S("g1tm", [128, C], BF16)
        qfm = S("qfm", [128, 8 * C], BF16)
        kfm = S("kfm", [128, 8 * C], BF16)
        g1fm = S("g1fm", [128, C], BF16)
        qhm = S("qhm", [64, H * C], BF16)
        khm = S("khm", [64, H * C], BF16)
        gate16 = S("gate16", [128, D], BF16)
        et16 = S("et16", [128, H * C], BF16)
        s_f32 = S("s_f32", [64, H * HD])
        s_16 = S("s_16", [64, H * HD], BF16)
        og_t = S("og_t", [128, D])
        lnat16 = S("lnat16", [128, D], BF16)
        lnafm = S("lnafm", [128, 8 * C], BF16)
        r2t = S("r2t", [128, D])
        ln2t = S("ln2t", [128, D])
        stat = S("stat", [128, 8])
        runmax = S("runmax", [128, 4])
        ceps = S("ceps", [128, 1])
        ffm = S("ffm", [128, 16 * D], BF16)
        sqdum = S("sqdum", [128, D], BF16)
        slab1 = S("slab1", [128, 2048])
        slab2 = S("slab2", [128, 2048])
        slab3 = S("slab3", [128, 2048])
        yqi = ffm.ap().bitcast(I16)[:, 14 * 1024:16 * 1024]
        o16s = xln16
        wtl = x_t.ap().bitcast(BF16)
        smax = og_t.ap()[0:4, 0:128]
        rbc = og_t.ap()[:, 128:144]
        wsc = og_t.ap()[:, 144:208]

        P_proj = es.enter_context(nc.psum_tensor("P_proj", [128, D], F32))
        P_et = es.enter_context(nc.psum_tensor("P_et", [128, 512], F32))
        P_o = es.enter_context(nc.psum_tensor("P_o", [128, D], F32))
        P_s = es.enter_context(nc.psum_tensor("P_s", [64, H * HD], F32))

        LDW = Sem(es, nc, "ldw")
        LDX = Sem(es, nc, "ldx")
        TRN = Sem(es, nc, "trn")
        STR = Sem(es, nc, "str")
        VEC = Sem(es, nc, "vec")
        SCL = Sem(es, nc, "scl")
        PE = Sem(es, nc, "pe")
        GPS = Sem(es, nc, "gps")
        CCS = Sem(es, nc, "ccs")

        block = es.enter_context(nc.Block())
        v, sc, pe, gp, sy = (RecEngine(), RecEngine(), RecEngine(),
                             RecEngine(), RecEngine())

        NT32 = NT * 32
        gp.iota(scr[:, 0:C], pattern=[[1, C]], base=0, channel_multiplier=0,
                allow_small_or_imprecise_dtypes=True)
        GPS.inc(gp.iota(scr[:, C:2 * C], pattern=[[0, C]], base=0,
                        channel_multiplier=1,
                        allow_small_or_imprecise_dtypes=True))
        v.memset(ceps[:], EPS)
        v.memset(g1tm[:, HD:C], 0.0)
        v.memset(stat[:], 0.0)
        v.memset(runmax[:], 0.0)
        GPS.wait(v)
        VEC.inc(v.tensor_tensor(mask16[:], scr[:, C:2 * C], scr[:, 0:C],
                                AL.is_le))
        VEC.wait(gp)
        gp.iota(scr[:, 0:NT32], pattern=[[128, NT], [0, 32]], base=0,
                channel_multiplier=1, allow_small_or_imprecise_dtypes=True)
        GPS.inc(gp.iota(scr[:, NT32:2 * NT32], pattern=[[0, NT], [1, 32]],
                        base=0, channel_multiplier=0,
                        allow_small_or_imprecise_dtypes=True))
        GPS.wait(sc)
        SCL.inc(sc.activation(slab1[:, 0:NT32], scr[:, NT32:2 * NT32],
                              AF.Exp, scale=-2.0 * math.log(10000.0) / HD))
        SCL.wait(v)
        VEC.inc(v.tensor_mul(scr[:, NT32:2 * NT32], scr[:, 0:NT32],
                             slab1[:, 0:NT32]))
        ang = scr[:, NT32:2 * NT32]
        krd = slab1.ap()[:, 0:NT32]
        wrp = scr[:, 0:NT32]
        VEC.wait(v)
        VEC.inc(v.tensor_scalar(krd, ang, 1.0 / TWO_PI, MAGIC, AL.mult, AL.add))
        VEC.wait(v)
        VEC.inc(v.tensor_single_scalar(krd, krd, MAGIC, AL.subtract))
        VEC.wait(v)
        VEC.inc(v.scalar_tensor_tensor(wrp, krd, -TWO_PI, ang, AL.mult, AL.add))
        VEC.wait(v)
        VEC.inc(v.tensor_scalar(wrp, wrp, PI, -PI, AL.min, AL.max))
        VEC.wait(sc)
        SCL.inc(sc.activation(sin16[:], wrp, AF.Sin))
        SCL.wait(v)
        VEC.inc(v.tensor_scalar(krd, wrp, PI / 2, TWO_PI, AL.is_gt, AL.mult))
        VEC.wait(v)
        VEC.inc(v.tensor_scalar(wrp, wrp, 1.0, PI / 2, AL.mult, AL.add))
        VEC.wait(v)
        VEC.inc(v.tensor_sub(wrp, wrp, krd))
        VEC.wait(v)
        VEC.inc(v.tensor_scalar(wrp, wrp, PI, -PI, AL.min, AL.max))
        VEC.wait(sc)
        SCL.inc(sc.activation(cos16[:], wrp, AF.Sin))
        SCL_TABLES = SCL.n

        for wdst, wsrc in ((wq16, Wq_h), (wk16, Wk_h), (wv16, Wv_h),
                           (wo16, Wo_h)):
            for kk in range(8):
                LDW.inc(gp.dma_start(out=wdst[:, kk * D:(kk + 1) * D],
                                     in_=wsrc[kk * 128:(kk + 1) * 128, :]), 16)
        for kk in range(8):
            LDW.inc(gp.dma_start(out=wg1_16[:, kk * HD:(kk + 1) * HD],
                                 in_=Wg1_h[kk * 128:(kk + 1) * 128, :]), 16)
        LDW.inc(gp.dma_start(out=wg2_16[:], in_=Wg2_h[:]), 16)
        SCL.wait(gp, SCL_TABLES)
        LDW.inc(gp.dma_start(out=gamma_r[:], in_=gam_h.broadcast_to([128, D])), 16)
        LDW.inc(gp.dma_start(out=beta_r[:], in_=bet_h.broadcast_to([128, D])), 16)
        LDW_ATTN = LDW.n

        def ln_stats_tile(src_ap):
            SCL.inc(sc.activation(sqdum[:], src_ap, AF.Square,
                                  accum_out=stat[:, 5:6]))
            VEC.inc(v.tensor_reduce(stat[:, 4:5], src_ap, AX, AL.add))
            SCL.wait(v)
            VEC.wait(v)
            VEC.inc(v.tensor_scalar(stat[:, 0:2], stat[:, 4:6], 1.0 / 1024.0,
                                    0.0, AL.mult, AL.add))
            VEC.wait(v)
            VEC.inc(v.scalar_tensor_tensor(stat[:, 2:3], stat[:, 0:1], 1.0,
                                           stat[:, 0:1], AL.mult, AL.mult))
            VEC.wait(v)
            VEC.inc(v.tensor_sub(stat[:, 2:3], stat[:, 1:2], stat[:, 2:3]))
            VEC.wait(v)
            VEC.inc(v.tensor_scalar_add(stat[:, 2:3], stat[:, 2:3], EPS))
            VEC.wait(sc)
            SCL.inc(sc.activation(stat[:, 3:4], stat[:, 2:3], AF.Sqrt))
            SCL.wait(v)
            VEC.inc(v.reciprocal(stat[:, 1:2], stat[:, 3:4]))
            VEC.wait(v)

        def rotary(src, dst):
            s3 = src.ap().rearrange("p (h two i) -> p h two i", h=H, two=2)
            d3 = dst.ap().rearrange("p (h two i) -> p h two i", h=H, two=2)
            s1, s2 = s3[:, :, 0], s3[:, :, 1]
            d1, d2 = d3[:, :, 0], d3[:, :, 1]
            cs = cos16[:, t * 32:(t + 1) * 32].rearrange(
                "p (one i) -> p one i", one=1).broadcast_to([128, H, 32])
            sn = sin16[:, t * 32:(t + 1) * 32].rearrange(
                "p (one i) -> p one i", one=1).broadcast_to([128, H, 32])
            tmp3 = lnat16.ap().rearrange("p (h two i) -> p h two i", h=H, two=2)
            v.tensor_tensor(d1, s1, cs, AL.mult)
            v.tensor_tensor(d2, s2, cs, AL.mult)
            v.tensor_tensor(tmp3[:, :, 0], s2, sn, AL.mult)
            VEC.inc(v.tensor_tensor(tmp3[:, :, 1], s1, sn, AL.mult))
            VEC.wait(v)
            v.tensor_tensor(d1, d1, tmp3[:, :, 0], AL.subtract)
            return v.tensor_tensor(d2, d2, tmp3[:, :, 1], AL.add)

        tile_end_vec = []
        for seq in range(NSEQ):
            if seq * NT >= max_tiles:
                break
            VEC.inc(v.memset(s_f32[:], 0.0))
            VEC.inc(v.memset(s_16[:], 0.0))
            for t in range(NT):
                gt = seq * NT + t
                if gt >= max_tiles:
                    break
                row0 = seq * T + t * C
                buf = gt % 2
                if gt >= 2:
                    VEC.wait(sy, tile_end_vec[gt - 2])
                LDX.inc(sy.dma_start(out=x_t[:, buf * D:(buf + 1) * D],
                                     in_=x_in[row0:row0 + 128, :]), 16)
                x_ap = x_t[:, buf * D:(buf + 1) * D]
                LDX.wait(v)
                LDX.wait(sc)
                if gt == 0:
                    LDW.wait(v, LDW_ATTN)
                    SCL.wait(v, SCL_TABLES)
                ln_stats_tile(x_ap)
                VEC.inc(v.tensor_scalar(xln16[:, buf * D:(buf + 1) * D], x_ap,
                                        stat[:, 0:1], stat[:, 1:2],
                                        AL.subtract, AL.mult))
                if gt == 0 and dbg_on:
                    VEC.wait(gp)
                    STR.inc(gp.dma_start(out=dbg_h[0:128, :],
                                         in_=xln16[:, buf * D:(buf + 1) * D]), 16)
                if stages < 2:
                    continue
                VEC.wait(sc)
                if gt == 0:
                    LDW.wait(sc, LDW_ATTN)
                for kk in range(8):
                    TRN.inc(sc.dma_start(
                        out=xfm[:, (buf * 8 + kk) * C:(buf * 8 + kk + 1) * C],
                        in_=xln16[:, buf * D + kk * C:buf * D + (kk + 1) * C],
                        transpose=True), 16)
                xfm_b = lambda kk: xfm[:, (buf * 8 + kk) * C:(buf * 8 + kk + 1) * C]
                if stages < 3:
                    continue
                TRN.wait(pe)
                for name, wsb in (("q", wq16), ("k", wk16), ("v", wv16)):
                    mm = None
                    for nn in range(2):
                        for kk in range(8):
                            mm = pe.matmul(
                                P_proj[:, nn * 512:(nn + 1) * 512], xfm_b(kk),
                                wsb[:, kk * D + nn * 512:kk * D + (nn + 1) * 512],
                                start=(kk == 0), stop=(kk == 7))
                    PE.inc(mm)
                    if name == "q":
                        PE.wait(sc)
                        SCL.inc(sc.activation(qtm[:], P_proj[:], AF.Silu))
                        SCL.wait(pe)
                    elif name == "k":
                        PE.wait(sc)
                        SCL.inc(sc.activation(ktm[:], P_proj[:], AF.Silu))
                        SCL.wait(pe)
                    else:
                        PE.wait(v)
                        VEC.inc(v.tensor_copy(vtm[:], P_proj[:]))
                        VEC.wait(pe)
                mm = None
                for kk in range(8):
                    mm = pe.matmul(P_proj[:, 0:HD], xfm_b(kk),
                                   wg1_16[:, kk * HD:(kk + 1) * HD],
                                   start=(kk == 0), stop=(kk == 7))
                PE.inc(mm)
                PE.wait(v)
                VEC.inc(v.tensor_copy(g1tm[:, 0:HD], P_proj[:, 0:HD]))
                VEC.wait(pe)
                if stages < 4:
                    continue
                SCL.wait(v)
                if gt == 0 and dbg_on:
                    SCL.wait(gp)
                    STR.inc(gp.dma_start(out=dbg_h[128:256, :], in_=qtm[:]), 16)
                    STR.inc(gp.dma_start(out=dbg_h[3 * 128:4 * 128, 0:32], in_=cos16[:, 0:32]), 16)
                    STR.inc(gp.dma_start(out=dbg_h[3 * 128:4 * 128, 32:64], in_=sin16[:, 0:32]), 16)
                    STR.inc(gp.dma_start(out=dbg_h[3 * 128:4 * 128, 64:128], in_=ktm[:, 0:64]), 16)
                li = rotary(qtm, qrot)
                VEC.inc(li)
                VEC.wait(v)
                li = rotary(ktm, krot)
                VEC.inc(li)
                if gt == 0 and dbg_on:
                    VEC.wait(gp)
                    STR.inc(gp.dma_start(out=dbg_h[2 * 128:3 * 128, :], in_=qrot[:]), 16)
                VEC.wait(sc)
                for kk in range(8):
                    TRN.inc(sc.dma_start(out=qfm[:, kk * C:(kk + 1) * C],
                                         in_=qrot[:, kk * C:(kk + 1) * C],
                                         transpose=True), 16)
                    TRN.inc(sc.dma_start(out=kfm[:, kk * C:(kk + 1) * C],
                                         in_=krot[:, kk * C:(kk + 1) * C],
                                         transpose=True), 16)
                TRN.inc(sc.dma_start(out=g1fm[:], in_=g1tm[:], transpose=True), 16)
                for srct, dstt in ((qfm, qhm), (kfm, khm)):
                    dv = dstt.ap().rearrange("p (j two c) -> p j two c", two=2, c=C)
                    TRN.inc(sc.dma_start(out=dv[:, :, 0], in_=srct[0:64, :]), 16)
                    TRN.inc(sc.dma_start(out=dv[:, :, 1], in_=srct[64:128, :]), 16)
                if stages < 5:
                    continue
                TRN.wait(pe)
                for grp in range(4):
                    mm = None
                    for hh in range(4):
                        h = grp * 4 + hh
                        mm = pe.matmul(P_et[:, hh * C:(hh + 1) * C],
                                       khm[:, h * C:(h + 1) * C],
                                       qhm[:, h * C:(h + 1) * C],
                                       start=True, stop=True)
                    PE.inc(mm)
                    PE.wait(v)
                    VEC.inc(v.tensor_tensor(
                        et16.ap().rearrange("p (h n) -> p h n", h=H)[
                            :, grp * 4:(grp + 1) * 4],
                        P_et.ap().rearrange("p (h n) -> p h n", h=4),
                        mask16.ap().rearrange("p (one n) -> p one n",
                                              one=1).broadcast_to([128, 4, C]),
                        AL.mult))
                    VEC.wait(pe)
                if gt == 0 and dbg_on:
                    VEC.wait(gp)
                    STR.inc(gp.dma_start(out=dbg_h[4 * 128:5 * 128, :],
                                         in_=et16[:, 0:D]), 16)
                if stages < 6:
                    continue
                mm = None
                for h in range(H):
                    j, b2 = h // 2, 64 * (h % 2)
                    mm = pe.matmul(P_o[:, h * HD:(h + 1) * HD],
                                   et16[:, h * C:(h + 1) * C],
                                   vtm[:, h * HD:(h + 1) * HD],
                                   start=True, stop=(t == 0))
                    if t > 0:
                        mm = pe.matmul(P_o[:, h * HD:(h + 1) * HD],
                                       qhm[:, h * C:(h + 1) * C],
                                       s_16[:, h * HD:(h + 1) * HD],
                                       start=False, stop=True)
                PE.inc(mm)
                PE_O = PE.n
                mm = None
                for nn in range(2):
                    mm = pe.matmul(P_proj[:, nn * 512:(nn + 1) * 512],
                                   g1fm[0:HD, :],
                                   wg2_16[:, nn * 512:(nn + 1) * 512],
                                   start=True, stop=True)
                PE.inc(mm)
                PE.wait(sc)
                SCL.inc(sc.activation(gate16[:], P_proj[:], AF.Sigmoid))
                SCL.wait(pe)
                mm = None
                for h in range(H):
                    mm = pe.matmul(P_s[:, h * HD:(h + 1) * HD],
                                   krot[:, h * HD:(h + 1) * HD],
                                   vtm[:, h * HD:(h + 1) * HD],
                                   start=True, stop=True)
                PE.inc(mm)
                PE.wait(v)
                VEC.inc(v.tensor_add(s_f32[:], s_f32[:], P_s[:]))
                VEC.wait(v)
                VEC.inc(v.tensor_copy(s_16[:], s_f32[:]))
                SCL.wait(v)
                VEC.inc(v.tensor_tensor(og_t[:], P_o[:], gate16[:], AL.mult))
                VEC.wait(v)
                VEC.wait(pe)
                if gt == 0 and dbg_on:
                    VEC.wait(gp)
                    STR.inc(gp.dma_start(out=dbg_h[5 * 128:6 * 128, :], in_=og_t[:]), 16)
                    STR.inc(gp.dma_start(out=dbg_h[7 * 128:8 * 128, :], in_=gate16[:]), 16)
                if stages < 7:
                    continue
                VEC.wait(sc)
                ln_stats_tile(og_t[:])
                VEC.inc(v.tensor_scalar(og_t[:], og_t[:], stat[:, 0:1],
                                        stat[:, 1:2], AL.subtract, AL.mult))
                VEC.wait(v)
                VEC.inc(v.tensor_tensor(og_t[:], og_t[:], gamma_r[:], AL.mult))
                VEC.wait(v)
                VEC.inc(v.tensor_tensor(lnat16[:], og_t[:], beta_r[:], AL.add))
                if gt == 0 and dbg_on:
                    VEC.wait(gp)
                    STR.inc(gp.dma_start(out=dbg_h[6 * 128:7 * 128, :], in_=lnat16[:]), 16)
                VEC.wait(sc)
                for kk in range(8):
                    TRN.inc(sc.dma_start(out=lnafm[:, kk * C:(kk + 1) * C],
                                         in_=lnat16[:, kk * C:(kk + 1) * C],
                                         transpose=True), 16)
                if stages < 8:
                    continue
                TRN.wait(pe)
                mm = None
                for nn in range(2):
                    for kk in range(8):
                        mm = pe.matmul(
                            P_proj[:, nn * 512:(nn + 1) * 512],
                            lnafm[:, kk * C:(kk + 1) * C],
                            wo16[:, kk * D + nn * 512:kk * D + (nn + 1) * 512],
                            start=(kk == 0), stop=(kk == 7))
                PE.inc(mm)
                PE.wait(v)
                VEC.inc(v.tensor_tensor(r2t[:], P_proj[:], x_ap, AL.add))
                VEC.wait(pe)
                VEC.wait(gp)
                STR.inc(gp.dma_start(out=resid2_h[row0:row0 + 128, :],
                                     in_=r2t[:]), 16)
                if phase == "attn":
                    STR.inc(gp.dma_start(out=out_h[row0:row0 + 128, :],
                                         in_=r2t[:]), 16)
                VEC.wait(v)
                VEC.wait(sc)
                ln_stats_tile(r2t[:])
                VEC.inc(v.tensor_scalar(ln2t[:], r2t[:], stat[:, 0:1],
                                        stat[:, 1:2], AL.subtract, AL.mult))
                VEC.wait(v)
                VEC.inc(v.tensor_reduce(stat[:, 6:7], ln2t[:], AX, AL.max,
                                        apply_absolute_value=True))
                VEC.wait(v)
                if gt == 0:
                    VEC.inc(v.tensor_copy(runmax[:, 0:1], stat[:, 6:7]))
                else:
                    VEC.inc(v.tensor_max(runmax[:, 0:1], runmax[:, 0:1],
                                         stat[:, 6:7]))
                VEC.wait(gp)
                STR.inc(gp.dma_start(out=ln2_h[row0:row0 + 128, :],
                                     in_=ln2t[:]), 16)
                STR.wait(v)
                tile_end_vec.append(VEC.n)

        if phase == "full":
            HT = NSEQ * T // 2
            NOC = FF // 128
            M = MAGIC

            def barrier(bi, cols):
                n = len(cols)
                VEC.wait(gp)
                for j, cc2 in enumerate(cols):
                    STR.inc(gp.dma_start(
                        out=pmax_h[j:j + 1, :].rearrange("one p -> p one"),
                        in_=runmax[:, cc2:cc2 + 1]), 16)
                STR.wait(sy)
                LDX.inc(sy.dma_start(out=smax[0:n, :], in_=pmax_h[0:n, :]), 16)
                LDX.wait(v)
                VEC.inc(v.tensor_reduce(smax[0:n, 0:1], smax[0:n, :], AX, AL.max))
                VEC.wait(gp)
                STR.inc(gp.dma_start(
                    out=cc_in[bi][0:1, 0:n].rearrange("one n -> n one"),
                    in_=smax[0:n, 0:1]), 16)
                STR.wait(gp)
                CCS.inc(gp.collective_compute(
                    "AllReduce", AL.max, replica_groups=[list(range(8))],
                    ins=[cc_in[bi]], outs=[cc_out[bi]]))
                CCS.wait(gp)
                for j in range(n):
                    LDX.inc(gp.dma_start(
                        out=rbc[:, 4 * bi + j:4 * bi + j + 1],
                        in_=cc_out[bi][0:1, j:j + 1].broadcast_to([128, 1])), 16)
                LDX.wait(v)

            def smul(dst, c1, c2):
                VEC.inc(v.tensor_mul(rbc[:, dst:dst + 1], rbc[:, c1:c1 + 1],
                                     rbc[:, c2:c2 + 1]))
                VEC.wait(v)

            def qscale(dst_s, dst_r, src_col, mul):
                VEC.inc(v.tensor_scalar(rbc[:, dst_s:dst_s + 1],
                                        rbc[:, src_col:src_col + 1],
                                        mul, 1e-12, AL.mult, AL.max))
                VEC.wait(v)
                VEC.inc(v.reciprocal(rbc[:, dst_r:dst_r + 1],
                                     rbc[:, dst_s:dst_s + 1]))
                VEC.wait(v)
                VEC.wait(sc)

            barrier(0, [0])
            qscale(1, 2, 0, 1.0 / QMAX_A)

            wdq16 = xln16.ap()[:, 0:1024]
            wtrp = qtm.ap()[:, 0:1024]
            dq3 = slab1.ap()[:, 0:1024].rearrange("p (b k) -> p b k", k=32)
            dq_jobs = []
            for wsrc, wdstT, nch, ncol in ((wup_h, wupT_h, NOC, 1024),
                                           (wgate_h, wgateT_h, NOC, 1024),
                                           (wdown_h, wdownT_h, 8, FF)):
                for ch in range(nch):
                    for cs in range(0, ncol, 1024):
                        dq_jobs.append((wsrc, wdstT, ch, cs))
            dq_vmark = {}
            dq_tmark = {}
            dq_smark = {}
            for ji, (wsrc, wdstT, ch, cs) in enumerate(dq_jobs):
                pp = ji % 2
                wstg = scr[:, 0:1024] if pp == 0 else slab2[:, 0:1024]
                w3 = wstg.rearrange("p (b k) -> p b k", k=32)
                if ji == 0:
                    VEC.wait(sy)
                elif ji >= 2:
                    VEC.wait(sy, dq_vmark[ji - 2])
                LDX.inc(sy.dma_start(
                    out=wstg,
                    in_=wsrc[ch * 128:(ch + 1) * 128, cs:cs + 1024]), 16)
                LDX.wait(v)
                VEC.inc(v.tensor_reduce(wsc[:, 0:32], w3, AX, AL.max,
                                        apply_absolute_value=True))
                VEC.wait(v)
                VEC.inc(v.tensor_scalar(wsc[:, 0:32], wsc[:, 0:32],
                                        1.0 / QMAX_W, 1e-12,
                                        AL.mult, AL.max))
                VEC.wait(v)
                VEC.inc(v.reciprocal(wsc[:, 32:64], wsc[:, 0:32]))
                VEC.wait(v)
                rb3 = wsc[:, 32:64].rearrange(
                    "p (b one) -> p b one", one=1).broadcast_to(
                        [128, 32, 32])
                sb3 = wsc[:, 0:32].rearrange(
                    "p (b one) -> p b one", one=1).broadcast_to(
                        [128, 32, 32])
                VEC.inc(v.tensor_tensor(dq3, w3, rb3, AL.mult))
                dq_vmark[ji] = VEC.n
                VEC.wait(v)
                VEC.inc(v.tensor_scalar(slab1[:, 0:1024],
                                        slab1[:, 0:1024], 1.0, M,
                                        AL.mult, AL.add))
                VEC.wait(v)
                VEC.inc(v.tensor_single_scalar(
                    slab1[:, 0:1024], slab1[:, 0:1024], M, AL.subtract))
                VEC.wait(v)
                if ji >= 1:
                    TRN.wait(v, dq_tmark[ji - 1])
                VEC.inc(v.tensor_tensor(
                    wdq16.rearrange("p (b k) -> p b k", k=32),
                    dq3, sb3, AL.mult))
                VEC.wait(sc)
                if ji >= 1:
                    STR.wait(sc, dq_smark[ji - 1])
                for kk in range(8):
                    TRN.inc(sc.dma_start(
                        out=wtrp[:, kk * 128:(kk + 1) * 128],
                        in_=wdq16[:, kk * 128:(kk + 1) * 128],
                        transpose=True), 16)
                dq_tmark[ji] = TRN.n
                TRN.wait(gp)
                for kk in range(8):
                    STR.inc(gp.dma_start(
                        out=wdstT[cs + kk * 128:cs + (kk + 1) * 128,
                                  ch * 128:(ch + 1) * 128],
                        in_=wtrp[:, kk * 128:(kk + 1) * 128]), 16)
                dq_smark[ji] = STR.n
            TRN.wait(v)
            STR.wait(v)
            TRN.wait(sy)
            STR.wait(sy)
            TRN.wait(gp)

            VEC.inc(v.memset(runmax[:, 1:3], 0.0))
            VEC.wait(v)
            pe_oc = {}
            for half in range(2):
                for tt2 in range(16):
                    row0 = half * HT + tt2 * 128
                    VEC.wait(sy)
                    LDX.inc(sy.dma_start(out=ln2t[:],
                                         in_=ln2_h[row0:row0 + 128, :]), 16)
                    LDX.wait(v)
                    VEC.inc(v.tensor_scalar(slab3[:, 0:1024], ln2t[:],
                                            rbc[:, 2:3], M, AL.mult, AL.add))
                    VEC.wait(v)
                    VEC.inc(v.tensor_single_scalar(
                        qrot.ap()[:, 0:1024], slab3[:, 0:1024], M, AL.subtract))
                    if dbg_on and half == 0 and tt2 == 0:
                        VEC.wait(gp)
                        STR.inc(gp.dma_start(out=dbg_h[128:256, :],
                                             in_=qrot.ap()[:, 0:1024]), 16)
                    VEC.wait(sc)
                    for kk in range(8):
                        TRN.inc(sc.dma_start(
                            out=ffm[:, (kk * 16 + tt2) * 128:
                                    (kk * 16 + tt2 + 1) * 128],
                            in_=qrot.ap()[:, kk * 128:(kk + 1) * 128],
                            transpose=True), 16)
                    TRN.wait(v)
                for oc in range(NOC):
                    wb = (oc % 2) * 2048
                    if oc >= 2:
                        PE.wait(sy, pe_oc[(half, oc - 2)])
                    LDX.inc(sy.dma_start(
                        out=wtl[:, wb:wb + 1024].rearrange(
                            "p (kk c) -> p kk c", c=128),
                        in_=wupT_h[:, oc * 128:(oc + 1) * 128].rearrange(
                            "(kk p) c -> p kk c", p=128)), 16)
                    LDX.inc(sy.dma_start(
                        out=wtl[:, wb + 1024:wb + 2048].rearrange(
                            "p (kk c) -> p kk c", c=128),
                        in_=wgateT_h[:, oc * 128:(oc + 1) * 128].rearrange(
                            "(kk p) c -> p kk c", p=128)), 16)
                    LDX.wait(pe)
                    if half == 0 and oc == 0:
                        TRN.wait(pe)
                        STR.wait(pe)
                    evac_marks = []
                    for sp in range(4):
                        bk = (sp % 2) * 512
                        if sp >= 2:
                            VEC.wait(pe, evac_marks[sp - 2][0])
                            SCL.wait(pe, evac_marks[sp - 2][1])
                        mmu = None
                        for kk in range(8):
                            mmu = pe.matmul(
                                P_proj[:, bk:bk + 512],
                                wtl[:, wb + kk * 128:wb + (kk + 1) * 128],
                                ffm[:, kk * 2048 + sp * 512:
                                    kk * 2048 + (sp + 1) * 512],
                                start=(kk == 0), stop=(kk == 7))
                        PE.inc(mmu)
                        PE.wait(v)
                        VEC.inc(v.tensor_copy(slab1[:, sp * 512:(sp + 1) * 512],
                                              P_proj[:, bk:bk + 512]))
                        mmg = None
                        for kk in range(8):
                            mmg = pe.matmul(
                                P_o[:, bk:bk + 512],
                                wtl[:, wb + 1024 + kk * 128:
                                    wb + 1024 + (kk + 1) * 128],
                                ffm[:, kk * 2048 + sp * 512:
                                    kk * 2048 + (sp + 1) * 512],
                                start=(kk == 0), stop=(kk == 7))
                        PE.inc(mmg)
                        PE.wait(sc)
                        SCL.inc(sc.copy(slab2[:, sp * 512:(sp + 1) * 512],
                                        P_o[:, bk:bk + 512]))
                        evac_marks.append((VEC.n, SCL.n))
                    VEC.wait(pe)
                    SCL.wait(pe)
                    pe_oc[(half, oc)] = PE.n
                    SCL.wait(v)
                    VEC.inc(v.tensor_reduce(stat[:, 6:7], slab1[:], AX, AL.max,
                                            apply_absolute_value=True))
                    VEC.wait(v)
                    VEC.inc(v.tensor_max(runmax[:, 1:2], runmax[:, 1:2],
                                         stat[:, 6:7]))
                    VEC.inc(v.tensor_reduce(stat[:, 7:8], slab2[:], AX, AL.max,
                                            apply_absolute_value=True))
                    VEC.wait(v)
                    VEC.inc(v.tensor_max(runmax[:, 2:3], runmax[:, 2:3],
                                         stat[:, 7:8]))
                    VEC.wait(gp)
                    SCL.wait(gp)
                    if dbg_on and half == 0 and oc == 0:
                        STR.inc(gp.dma_start(out=dbg_h[256:384, :],
                                             in_=slab1[:, 0:1024]), 16)
                    STR.inc(gp.dma_start(
                        out=upraw_h[oc * 128:(oc + 1) * 128,
                                    half * HT:(half + 1) * HT],
                        in_=slab1[:]), 16)
                    STR.inc(gp.dma_start(
                        out=gateraw_h[oc * 128:(oc + 1) * 128,
                                      half * HT:(half + 1) * HT],
                        in_=slab2[:]), 16)
                    STR.wait(v)
                    STR.wait(sc)

            barrier(1, [1, 2])
            smul(4, 4, 1)
            smul(5, 5, 1)
            qscale(6, 7, 4, 1.0 / QMAX_A)
            qscale(9, 10, 5, 1.0 / QMAX_A)
            smul(7, 7, 1)
            smul(10, 10, 1)
            VEC.inc(v.tensor_scalar_mul(rbc[:, 11:12], rbc[:, 9:10],
                                        1.0 / 65536.0))
            VEC.wait(v)
            VEC.inc(v.memset(runmax[:, 3:4], 0.0))
            VEC.wait(v)
            VEC.wait(sc)

            HH = 1024
            steps = [(h2, oc2, hs) for h2 in range(2) for oc2 in range(NOC)
                     for hs in range(2)]
            for half in range(2):
                for oc in range(NOC):
                    VEC.wait(sy)
                    SCL.wait(sy)
                    LDX.inc(sy.dma_start(
                        out=slab1[:],
                        in_=upraw_h[oc * 128:(oc + 1) * 128,
                                    half * HT:(half + 1) * HT]), 16)
                    LDX.inc(sy.dma_start(
                        out=slab2[:],
                        in_=gateraw_h[oc * 128:(oc + 1) * 128,
                                      half * HT:(half + 1) * HT]), 16)
                    LDX.wait(v)
                    VEC.inc(v.tensor_scalar(slab3[:], slab1[:], rbc[:, 7:8], M,
                                            AL.mult, AL.add))
                    VEC.wait(v)
                    VEC.inc(v.tensor_single_scalar(yqi[:], slab3[:], M,
                                                   AL.subtract))
                    VEC.wait(gp)
                    STR.inc(gp.dma_start(
                        out=yqup_h[oc * 128:(oc + 1) * 128,
                                   half * HT:(half + 1) * HT],
                        in_=yqi[:]), 16)
                    VEC.wait(v)
                    VEC.inc(v.tensor_scalar(slab3[:], slab2[:], rbc[:, 10:11],
                                            M, AL.mult, AL.add))
                    VEC.wait(v)
                    VEC.inc(v.tensor_single_scalar(slab1[:], slab3[:], M,
                                                   AL.subtract))
                    VEC.wait(sc)
                    SCL.inc(sc.activation(slab2[:], slab1[:], AF.Sigmoid,
                                          scale=rbc[:, 9:10]))
                    SCL.wait(v)
                    VEC.inc(v.tensor_scalar(slab3[:], slab2[:], 65536.0, M,
                                            AL.mult, AL.add))
                    VEC.wait(v)
                    VEC.inc(v.tensor_scalar(slab3[:], slab3[:], M, 65535.0,
                                            AL.subtract, AL.min))
                    VEC.wait(v)
                    VEC.inc(v.tensor_mul(slab2[:], slab1[:], slab3[:]))
                    VEC.wait(v)
                    if dbg_on and half == 0 and oc == 0:
                        VEC.wait(gp)
                        STR.inc(gp.dma_start(out=dbg_h[384:512, :],
                                             in_=slab1[:, 0:1024]), 16)
                        STR.inc(gp.dma_start(out=dbg_h[512:640, :],
                                             in_=slab3[:, 0:1024]), 16)
                        STR.inc(gp.dma_start(out=dbg_h[640:768, :],
                                             in_=slab2[:, 0:1024]), 16)
                    VEC.inc(v.tensor_reduce(stat[:, 6:7], slab2[:], AX, AL.max,
                                            apply_absolute_value=True))
                    VEC.wait(v)
                    VEC.inc(v.tensor_max(runmax[:, 3:4], runmax[:, 3:4],
                                         stat[:, 6:7]))
                    VEC.wait(gp)
                    STR.inc(gp.dma_start(
                        out=p_h[oc * 128:(oc + 1) * 128,
                                half * HT:(half + 1) * HT],
                        in_=slab2[:]), 16)
                    STR.wait(v)

            barrier(2, [3])
            smul(13, 11, 8)
            qscale(13, 14, 13, 1.0 / QMAX_A)
            smul(3, 11, 14)

            VEC.inc(v.memset(runmax[:, 0:1], 0.0))
            VEC.wait(v)
            f4_mark = {}
            for si, (half, oc, hs) in enumerate(steps):
                cb = hs * HH
                s1h = slab1.ap()[:, cb:cb + HH]
                s2h = slab2.ap()[:, cb:cb + HH]
                s3h = slab3.ap()[:, cb:cb + HH]
                yqh = yqi[:, cb:cb + HH]
                c0 = half * HT + hs * HH
                if si >= 2:
                    VEC.wait(sy, f4_mark[si - 2][0])
                    STR.wait(sy, f4_mark[si - 2][1])
                LDX.inc(sy.dma_start(
                    out=s1h, in_=p_h[oc * 128:(oc + 1) * 128, c0:c0 + HH]), 16)
                LDX.inc(sy.dma_start(
                    out=yqh, in_=yqup_h[oc * 128:(oc + 1) * 128,
                                        c0:c0 + HH]), 16)
                LDX.wait(v)
                VEC.inc(v.tensor_scalar(s3h, s1h, rbc[:, 3:4], M,
                                        AL.mult, AL.add))
                VEC.wait(v)
                VEC.inc(v.scalar_tensor_tensor(s2h, s3h, M, yqh,
                                               AL.subtract, AL.mult))
                VEC.wait(v)
                VEC.inc(v.tensor_reduce(stat[:, 6:7], s2h, AX, AL.max,
                                        apply_absolute_value=True))
                VEC.wait(v)
                VEC.inc(v.tensor_max(runmax[:, 0:1], runmax[:, 0:1],
                                     stat[:, 6:7]))
                VEC.wait(gp)
                STR.inc(gp.dma_start(
                    out=m2_h[oc * 128:(oc + 1) * 128, c0:c0 + HH],
                    in_=s2h), 16)
                f4_mark[si] = (VEC.n, STR.n)

            barrier(3, [0])
            smul(15, 13, 6)
            smul(2, 15, 12)
            qscale(1, 2, 2, 1.0 / QMAX_A)
            smul(11, 15, 2)

            f5_mark = {}
            for si, (half, oc, hs) in enumerate(steps):
                cb = hs * HH
                s1h = slab1.ap()[:, cb:cb + HH]
                s3h = slab3.ap()[:, cb:cb + HH]
                o16h = o16s.ap()[:, cb:cb + HH]
                c0 = half * HT + hs * HH
                if si >= 2:
                    VEC.wait(sy, f5_mark[si - 2][0])
                    STR.wait(sy, f5_mark[si - 2][1])
                LDX.inc(sy.dma_start(
                    out=s1h, in_=m2_h[oc * 128:(oc + 1) * 128,
                                      c0:c0 + HH]), 16)
                LDX.wait(v)
                VEC.inc(v.tensor_scalar(s3h, s1h, rbc[:, 11:12], M,
                                        AL.mult, AL.add))
                VEC.wait(v)
                VEC.inc(v.tensor_single_scalar(o16h, s3h, M, AL.subtract))
                VEC.wait(gp)
                STR.inc(gp.dma_start(
                    out=o_h[oc * 128:(oc + 1) * 128, c0:c0 + HH],
                    in_=o16h), 16)
                f5_mark[si] = (VEC.n, STR.n)
            PE.wait(sy)
            wd_sb = [wq16, wk16, wv16]
            for kk in range(NOC):
                LDX.inc(sy.dma_start(
                    out=wd_sb[kk // 8][:, (kk % 8) * 1024:(kk % 8 + 1) * 1024],
                    in_=wdownT_h[kk * 128:(kk + 1) * 128, :]), 16)
            STR.wait(sy)
            tile_pe = {}
            for gt2 in range(NTILES):
                fb = (gt2 % 2) * NOC * 128
                if gt2 >= 2:
                    PE.wait(sy, tile_pe[gt2 - 2])
                LDX.inc(sy.dma_start(
                    out=ffm.ap()[:, fb:fb + NOC * 128].rearrange(
                        "p (kk c) -> p kk c", c=128),
                    in_=o_h[:, gt2 * 128:(gt2 + 1) * 128].rearrange(
                        "(kk p) c -> p kk c", p=128)), 16)
                LDX.wait(pe)
                mm = None
                for nn in range(2):
                    for kk in range(NOC):
                        wdt = wd_sb[kk // 8][:, (kk % 8) * 1024 + nn * 512:
                                             (kk % 8) * 1024 + (nn + 1) * 512]
                        mm = pe.matmul(
                            P_proj[:, nn * 512:(nn + 1) * 512],
                            ffm[:, fb + kk * 128:fb + (kk + 1) * 128],
                            wdt, start=(kk == 0), stop=(kk == NOC - 1))
                PE.inc(mm)
                tile_pe[gt2] = PE.n
                row0 = gt2 * 128
                VEC.wait(sy)
                LDX.inc(sy.dma_start(out=r2t[:],
                                     in_=resid2_h[row0:row0 + 128, :]), 16)
                LDX.wait(v)
                PE.wait(v)
                VEC.inc(v.scalar_tensor_tensor(ln2t[:], P_proj[:],
                                               rbc[:, 1:2], r2t[:],
                                               AL.mult, AL.add))
                VEC.wait(pe)
                VEC.wait(gp)
                STR.inc(gp.dma_start(out=out_h[row0:row0 + 128, :],
                                     in_=ln2t[:]), 16)
                STR.wait(v)

        STR.wait(gp)

        block.vector(lambda e: v.replay(e))
        block.scalar(lambda e: sc.replay(e))
        block.tensor(lambda e: pe.replay(e))
        block.gpsimd(lambda e: gp.replay(e))
        block.sync(lambda e: sy.replay(e))
    return nc


import numpy as np
from concourse.bass_utils import run_bass_kernel_spmd

_NC_CACHE = {}


def kernel(L0_attn_x, L0_attn_Wq, L0_attn_Wk, L0_attn_Wv, L0_attn_Wo,
           L0_attn_Wg1, L0_attn_Wg2, L0_attn_gamma, L0_attn_beta,
           L0_ffn_w_gate, L0_ffn_w_up, L0_ffn_w_down):
    if "nc" not in _NC_CACHE:
        _NC_CACHE["nc"] = build("full")
    nc = _NC_CACHE["nc"]
    x = np.asarray(L0_attn_x, dtype=np.float32)
    common = dict(
        Wq=np.ascontiguousarray(L0_attn_Wq, dtype=np.float32),
        Wk=np.ascontiguousarray(L0_attn_Wk, dtype=np.float32),
        Wv=np.ascontiguousarray(L0_attn_Wv, dtype=np.float32),
        Wo=np.ascontiguousarray(L0_attn_Wo, dtype=np.float32),
        Wg1=np.ascontiguousarray(L0_attn_Wg1, dtype=np.float32),
        Wg2=np.ascontiguousarray(L0_attn_Wg2, dtype=np.float32),
        gamma=np.ascontiguousarray(np.asarray(L0_attn_gamma, dtype=np.float32)
                                   .reshape(1, D)),
        beta=np.ascontiguousarray(np.asarray(L0_attn_beta, dtype=np.float32)
                                  .reshape(1, D)),
        w_gate=np.ascontiguousarray(L0_ffn_w_gate, dtype=np.float32),
        w_up=np.ascontiguousarray(L0_ffn_w_up, dtype=np.float32),
        w_down=np.ascontiguousarray(L0_ffn_w_down, dtype=np.float32),
    )
    in_maps = []
    for c in range(8):
        shard = np.ascontiguousarray(
            x[NSEQ * c:NSEQ * (c + 1)].reshape(NSEQ * T, D))
        m = dict(common)
        m["x"] = shard
        in_maps.append(m)
    res = run_bass_kernel_spmd(nc, in_maps, core_ids=list(range(8)))
    out = np.concatenate(
        [np.asarray(res.results[c]["out"]).reshape(NSEQ, T, D)
         for c in range(8)], axis=0)
    return out


# version 23 (final)
# speedup vs baseline: 1.1993x; 1.1993x over previous
import math
import contextlib

import concourse.bass as bass
import concourse.mybir as mybir

F32 = mybir.dt.float32
BF16 = mybir.dt.bfloat16
I16 = mybir.dt.int16
AF = mybir.ActivationFunctionType
AL = mybir.AluOpType
AX = mybir.AxisListType.X

MAGIC = 12582912.0
TWO_PI = 6.283185307179586
PI = 3.141592653589793

NSEQ, T, D, H, HD, C = 2, 2048, 1024, 16, 64, 128
NT = T // C
NTILES = NSEQ * NT
FF = 3072
EPS = 1e-5
QMAX_A = 32767.0
QMAX_W = 7.0


class RecInst:
    __slots__ = ("thens",)

    def __init__(self):
        self.thens = []

    def then_inc(self, sem, k=1):
        self.thens.append((sem, k))
        return self


class RecEngine:

    def __init__(self):
        self.calls = []

    def __getattr__(self, name):
        def call(*a, **k):
            ri = RecInst()
            self.calls.append((name, a, k, ri))
            return ri
        return call

    def replay(self, eng):
        for name, a, k, ri in self.calls:
            inst = getattr(eng, name)(*a, **k)
            for sem, n in ri.thens:
                inst = inst.then_inc(sem, n)


class Sem:
    def __init__(self, es, nc, name):
        self.h = es.enter_context(nc.semaphore(name))
        self.n = 0

    def inc(self, inst, k=1):
        inst.then_inc(self.h, k)
        self.n += k
        return self.n

    def wait(self, eng, val=None):
        eng.wait_ge(self.h, self.n if val is None else val)


def build(phase="full", max_tiles=64, stages=99, dbg_on=False):
    nc = bass.Bass("TRN2", target_bir_lowering=False, debug=False,
                   detect_race_conditions=False)

    x_in = nc.dram_tensor("x", [NSEQ * T, D], F32, kind="ExternalInput").ap()
    Wq_h = nc.dram_tensor("Wq", [D, D], F32, kind="ExternalInput").ap()
    Wk_h = nc.dram_tensor("Wk", [D, D], F32, kind="ExternalInput").ap()
    Wv_h = nc.dram_tensor("Wv", [D, D], F32, kind="ExternalInput").ap()
    Wo_h = nc.dram_tensor("Wo", [D, D], F32, kind="ExternalInput").ap()
    Wg1_h = nc.dram_tensor("Wg1", [D, HD], F32, kind="ExternalInput").ap()
    Wg2_h = nc.dram_tensor("Wg2", [HD, D], F32, kind="ExternalInput").ap()
    gam_h = nc.dram_tensor("gamma", [1, D], F32, kind="ExternalInput").ap()
    bet_h = nc.dram_tensor("beta", [1, D], F32, kind="ExternalInput").ap()
    wgate_h = nc.dram_tensor("w_gate", [FF, D], F32, kind="ExternalInput").ap()
    wup_h = nc.dram_tensor("w_up", [FF, D], F32, kind="ExternalInput").ap()
    wdown_h = nc.dram_tensor("w_down", [D, FF], F32, kind="ExternalInput").ap()
    out_h = nc.dram_tensor("out", [NSEQ * T, D], F32, kind="ExternalOutput").ap()
    dbg_h = (nc.dram_tensor("dbg", [8 * 128, D], F32, kind="ExternalOutput").ap()
             if dbg_on else None)

    resid2_h = nc.dram_tensor("resid2_h", [NSEQ * T, D], F32).ap()
    ln2_h = nc.dram_tensor("ln2_h", [NSEQ * T, D], F32).ap()
    pmax_h = nc.dram_tensor("pmax_h", [4, 128], F32).ap()
    wupT_h = nc.dram_tensor("wupT_h", [D, FF], BF16).ap()
    wgateT_h = nc.dram_tensor("wgateT_h", [D, FF], BF16).ap()
    wdownT_h = nc.dram_tensor("wdownT_h", [FF, D], BF16).ap()
    o_h = nc.dram_tensor("o_h", [FF, NSEQ * T], BF16).ap()
    upraw_h = nc.dram_tensor("upraw_h", [FF, NSEQ * T], F32).ap()
    gateraw_h = nc.dram_tensor("gateraw_h", [FF, NSEQ * T], F32).ap()
    yqup_h = nc.dram_tensor("yqup_h", [FF, NSEQ * T], I16).ap()
    p_h = nc.dram_tensor("p_h", [FF, NSEQ * T], F32).ap()
    m2_h = nc.dram_tensor("m2_h", [FF, NSEQ * T], F32).ap()
    cc_in = [nc.dram_tensor(f"cc_in{i}", [1, 4], F32).ap() for i in range(4)]
    cc_out = [nc.dram_tensor(f"cc_out{i}", [1, 4], F32).ap() for i in range(4)]

    es = contextlib.ExitStack()
    with es:
        S = lambda n, shp, dt=F32: es.enter_context(nc.sbuf_tensor(n, shp, dt))

        mask16 = S("mask16", [C, C], BF16)
        cos16 = S("cos16", [128, NT * 32], BF16)
        sin16 = S("sin16", [128, NT * 32], BF16)
        gamma_r = S("gamma_r", [128, D])
        beta_r = S("beta_r", [128, D])
        wq16 = S("wq16", [128, 8 * D], BF16)
        wk16 = S("wk16", [128, 8 * D], BF16)
        wv16 = S("wv16", [128, 8 * D], BF16)
        wo16 = S("wo16", [128, 8 * D], BF16)
        wg1_16 = S("wg1_16", [128, 8 * HD], BF16)
        wg2_16 = S("wg2_16", [HD, D], BF16)
        scr = S("scr", [128, 1024])

        x_t = S("x_t", [128, 2 * D])
        xln16 = S("xln16", [128, 2 * D], BF16)
        xfm = S("xfm", [128, 8 * C * 2], BF16)
        qtm = S("qtm", [128, D], BF16)
        ktm = S("ktm", [128, D], BF16)
        qrot = S("qrot", [128, D], BF16)
        krot = S("krot", [128, D], BF16)
        vtm = S("vtm", [128, D], BF16)
        g1tm = S("g1tm", [128, C], BF16)
        qfm = S("qfm", [128, 8 * C], BF16)
        kfm = S("kfm", [128, 8 * C], BF16)
        g1fm = S("g1fm", [128, C], BF16)
        qhm = S("qhm", [64, H * C], BF16)
        khm = S("khm", [64, H * C], BF16)
        gate16 = S("gate16", [128, D], BF16)
        et16 = S("et16", [128, H * C], BF16)
        s_f32 = S("s_f32", [64, H * HD])
        s_16 = S("s_16", [64, H * HD], BF16)
        og_t = S("og_t", [128, D])
        lnat16 = S("lnat16", [128, D], BF16)
        lnafm = S("lnafm", [128, 8 * C], BF16)
        r2t = S("r2t", [128, D])
        ln2t = S("ln2t", [128, D])
        stat = S("stat", [128, 8])
        runmax = S("runmax", [128, 4])
        ceps = S("ceps", [128, 1])
        ffm = S("ffm", [128, 16 * D], BF16)
        sqdum = S("sqdum", [128, D], BF16)
        slab1 = S("slab1", [128, 2048])
        slab2 = S("slab2", [128, 2048])
        slab3 = S("slab3", [128, 2048])
        yqi = ffm.ap().bitcast(I16)[:, 14 * 1024:16 * 1024]
        o16s = xln16
        wtl = x_t.ap().bitcast(BF16)
        smax = og_t.ap()[0:4, 0:128]
        rbc = og_t.ap()[:, 128:144]
        wsc = og_t.ap()[:, 144:208]

        P_proj = es.enter_context(nc.psum_tensor("P_proj", [128, D], F32))
        P_et = es.enter_context(nc.psum_tensor("P_et", [128, 512], F32))
        P_o = es.enter_context(nc.psum_tensor("P_o", [128, D], F32))
        P_s = es.enter_context(nc.psum_tensor("P_s", [64, H * HD], F32))

        LDW = Sem(es, nc, "ldw")
        LDX = Sem(es, nc, "ldx")
        TRN = Sem(es, nc, "trn")
        STR = Sem(es, nc, "str")
        VEC = Sem(es, nc, "vec")
        SCL = Sem(es, nc, "scl")
        PE = Sem(es, nc, "pe")
        GPS = Sem(es, nc, "gps")
        CCS = Sem(es, nc, "ccs")

        block = es.enter_context(nc.Block())
        v, sc, pe, gp, sy = (RecEngine(), RecEngine(), RecEngine(),
                             RecEngine(), RecEngine())

        NT32 = NT * 32
        gp.iota(scr[:, 0:C], pattern=[[1, C]], base=0, channel_multiplier=0,
                allow_small_or_imprecise_dtypes=True)
        GPS.inc(gp.iota(scr[:, C:2 * C], pattern=[[0, C]], base=0,
                        channel_multiplier=1,
                        allow_small_or_imprecise_dtypes=True))
        v.memset(ceps[:], EPS)
        v.memset(g1tm[:, HD:C], 0.0)
        v.memset(stat[:], 0.0)
        v.memset(runmax[:], 0.0)
        GPS.wait(v)
        VEC.inc(v.tensor_tensor(mask16[:], scr[:, C:2 * C], scr[:, 0:C],
                                AL.is_le))
        VEC.wait(gp)
        gp.iota(scr[:, 0:NT32], pattern=[[128, NT], [0, 32]], base=0,
                channel_multiplier=1, allow_small_or_imprecise_dtypes=True)
        GPS.inc(gp.iota(scr[:, NT32:2 * NT32], pattern=[[0, NT], [1, 32]],
                        base=0, channel_multiplier=0,
                        allow_small_or_imprecise_dtypes=True))
        GPS.wait(sc)
        SCL.inc(sc.activation(slab1[:, 0:NT32], scr[:, NT32:2 * NT32],
                              AF.Exp, scale=-2.0 * math.log(10000.0) / HD))
        SCL.wait(v)
        VEC.inc(v.tensor_mul(scr[:, NT32:2 * NT32], scr[:, 0:NT32],
                             slab1[:, 0:NT32]))
        ang = scr[:, NT32:2 * NT32]
        krd = slab1.ap()[:, 0:NT32]
        wrp = scr[:, 0:NT32]
        VEC.wait(v)
        VEC.inc(v.tensor_scalar(krd, ang, 1.0 / TWO_PI, MAGIC, AL.mult, AL.add))
        VEC.wait(v)
        VEC.inc(v.tensor_single_scalar(krd, krd, MAGIC, AL.subtract))
        VEC.wait(v)
        VEC.inc(v.scalar_tensor_tensor(wrp, krd, -TWO_PI, ang, AL.mult, AL.add))
        VEC.wait(v)
        VEC.inc(v.tensor_scalar(wrp, wrp, PI, -PI, AL.min, AL.max))
        VEC.wait(sc)
        SCL.inc(sc.activation(sin16[:], wrp, AF.Sin))
        SCL.wait(v)
        VEC.inc(v.tensor_scalar(krd, wrp, PI / 2, TWO_PI, AL.is_gt, AL.mult))
        VEC.wait(v)
        VEC.inc(v.tensor_scalar(wrp, wrp, 1.0, PI / 2, AL.mult, AL.add))
        VEC.wait(v)
        VEC.inc(v.tensor_sub(wrp, wrp, krd))
        VEC.wait(v)
        VEC.inc(v.tensor_scalar(wrp, wrp, PI, -PI, AL.min, AL.max))
        VEC.wait(sc)
        SCL.inc(sc.activation(cos16[:], wrp, AF.Sin))
        SCL_TABLES = SCL.n

        for wdst, wsrc in ((wq16, Wq_h), (wk16, Wk_h), (wv16, Wv_h),
                           (wo16, Wo_h)):
            for kk in range(8):
                LDW.inc(gp.dma_start(out=wdst[:, kk * D:(kk + 1) * D],
                                     in_=wsrc[kk * 128:(kk + 1) * 128, :]), 16)
        for kk in range(8):
            LDW.inc(gp.dma_start(out=wg1_16[:, kk * HD:(kk + 1) * HD],
                                 in_=Wg1_h[kk * 128:(kk + 1) * 128, :]), 16)
        LDW.inc(gp.dma_start(out=wg2_16[:], in_=Wg2_h[:]), 16)
        SCL.wait(gp, SCL_TABLES)
        LDW.inc(gp.dma_start(out=gamma_r[:], in_=gam_h.broadcast_to([128, D])), 16)
        LDW.inc(gp.dma_start(out=beta_r[:], in_=bet_h.broadcast_to([128, D])), 16)
        LDW_ATTN = LDW.n

        def ln_stats_tile(src_ap):
            SCL.inc(sc.activation(sqdum[:], src_ap, AF.Square,
                                  accum_out=stat[:, 5:6]))
            VEC.inc(v.tensor_reduce(stat[:, 4:5], src_ap, AX, AL.add))
            SCL.wait(v)
            VEC.wait(v)
            VEC.inc(v.tensor_scalar(stat[:, 0:2], stat[:, 4:6], 1.0 / 1024.0,
                                    0.0, AL.mult, AL.add))
            VEC.wait(v)
            VEC.inc(v.scalar_tensor_tensor(stat[:, 2:3], stat[:, 0:1], 1.0,
                                           stat[:, 0:1], AL.mult, AL.mult))
            VEC.wait(v)
            VEC.inc(v.tensor_sub(stat[:, 2:3], stat[:, 1:2], stat[:, 2:3]))
            VEC.wait(sc)
            SCL.inc(sc.activation(stat[:, 3:4], stat[:, 2:3], AF.Sqrt,
                                  bias=ceps[:]))
            SCL.wait(v)
            VEC.inc(v.reciprocal(stat[:, 1:2], stat[:, 3:4]))
            VEC.wait(v)

        def rotary(src, dst):
            s3 = src.ap().rearrange("p (h two i) -> p h two i", h=H, two=2)
            d3 = dst.ap().rearrange("p (h two i) -> p h two i", h=H, two=2)
            s1, s2 = s3[:, :, 0], s3[:, :, 1]
            d1, d2 = d3[:, :, 0], d3[:, :, 1]
            cs = cos16[:, t * 32:(t + 1) * 32].rearrange(
                "p (one i) -> p one i", one=1).broadcast_to([128, H, 32])
            sn = sin16[:, t * 32:(t + 1) * 32].rearrange(
                "p (one i) -> p one i", one=1).broadcast_to([128, H, 32])
            tmp3 = lnat16.ap().rearrange("p (h two i) -> p h two i", h=H, two=2)
            v.tensor_tensor(d1, s1, cs, AL.mult)
            v.tensor_tensor(d2, s2, cs, AL.mult)
            v.tensor_tensor(tmp3[:, :, 0], s2, sn, AL.mult)
            VEC.inc(v.tensor_tensor(tmp3[:, :, 1], s1, sn, AL.mult))
            VEC.wait(v)
            v.tensor_tensor(d1, d1, tmp3[:, :, 0], AL.subtract)
            return v.tensor_tensor(d2, d2, tmp3[:, :, 1], AL.add)

        tile_end_vec = []
        for seq in range(NSEQ):
            if seq * NT >= max_tiles:
                break
            VEC.inc(v.memset(s_f32[:], 0.0))
            VEC.inc(v.memset(s_16[:], 0.0))
            for t in range(NT):
                gt = seq * NT + t
                if gt >= max_tiles:
                    break
                row0 = seq * T + t * C
                buf = gt % 2
                if gt >= 2:
                    VEC.wait(sy, tile_end_vec[gt - 2])
                TRN.wait(sy)
                LDX.inc(sy.dma_start(out=x_t[:, buf * D:(buf + 1) * D],
                                     in_=x_in[row0:row0 + 128, :]), 16)
                x_ap = x_t[:, buf * D:(buf + 1) * D]
                LDX.wait(v)
                LDX.wait(sc)
                if gt == 0:
                    LDW.wait(v, LDW_ATTN)
                    SCL.wait(v, SCL_TABLES)
                ln_stats_tile(x_ap)
                VEC.inc(v.tensor_scalar(xln16[:, buf * D:(buf + 1) * D], x_ap,
                                        stat[:, 0:1], stat[:, 1:2],
                                        AL.subtract, AL.mult))
                if gt == 0 and dbg_on:
                    VEC.wait(gp)
                    STR.inc(gp.dma_start(out=dbg_h[0:128, :],
                                         in_=xln16[:, buf * D:(buf + 1) * D]), 16)
                if stages < 2:
                    continue
                VEC.wait(sc)
                LDX.wait(sc)
                STR.wait(sc)
                if gt == 0:
                    LDW.wait(sc, LDW_ATTN)
                TRN.inc(sc.dma_start(
                    out=xfm.ap()[:, buf * 8 * C:(buf + 1) * 8 * C].rearrange(
                        "p (b c) -> p b c", c=C),
                    in_=xln16.ap()[:, buf * D:(buf + 1) * D].rearrange(
                        "p (b c) -> p b c", c=C),
                    transpose=True), 16)
                xfm_b = lambda kk: xfm[:, (buf * 8 + kk) * C:(buf * 8 + kk + 1) * C]
                if stages < 3:
                    continue
                TRN.wait(pe)
                for name, wsb in (("q", wq16), ("k", wk16), ("v", wv16)):
                    mm = None
                    for nn in range(2):
                        for kk in range(8):
                            mm = pe.matmul(
                                P_proj[:, nn * 512:(nn + 1) * 512], xfm_b(kk),
                                wsb[:, kk * D + nn * 512:kk * D + (nn + 1) * 512],
                                start=(kk == 0), stop=(kk == 7))
                    PE.inc(mm)
                    if name == "q":
                        PE.wait(sc)
                        SCL.inc(sc.activation(qtm[:], P_proj[:], AF.Silu))
                        SCL.wait(pe)
                    elif name == "k":
                        PE.wait(sc)
                        SCL.inc(sc.activation(ktm[:], P_proj[:], AF.Silu))
                        SCL.wait(pe)
                    else:
                        PE.wait(v)
                        VEC.inc(v.tensor_copy(vtm[:], P_proj[:]))
                        VEC.wait(pe)
                mm = None
                for kk in range(8):
                    mm = pe.matmul(P_proj[:, 0:HD], xfm_b(kk),
                                   wg1_16[:, kk * HD:(kk + 1) * HD],
                                   start=(kk == 0), stop=(kk == 7))
                PE.inc(mm)
                PE.wait(v)
                VEC.inc(v.tensor_copy(g1tm[:, 0:HD], P_proj[:, 0:HD]))
                VEC.wait(pe)
                if stages < 4:
                    continue
                SCL.wait(v)
                if gt == 0 and dbg_on:
                    SCL.wait(gp)
                    STR.inc(gp.dma_start(out=dbg_h[128:256, :], in_=qtm[:]), 16)
                    STR.inc(gp.dma_start(out=dbg_h[3 * 128:4 * 128, 0:32], in_=cos16[:, 0:32]), 16)
                    STR.inc(gp.dma_start(out=dbg_h[3 * 128:4 * 128, 32:64], in_=sin16[:, 0:32]), 16)
                    STR.inc(gp.dma_start(out=dbg_h[3 * 128:4 * 128, 64:128], in_=ktm[:, 0:64]), 16)
                li = rotary(qtm, qrot)
                VEC.inc(li)
                VEC.wait(v)
                li = rotary(ktm, krot)
                VEC.inc(li)
                if gt == 0 and dbg_on:
                    VEC.wait(gp)
                    STR.inc(gp.dma_start(out=dbg_h[2 * 128:3 * 128, :], in_=qrot[:]), 16)
                VEC.wait(sc)
                LDX.wait(sc)
                STR.wait(sc)
                TRN.inc(sc.dma_start(
                    out=qfm.ap().rearrange("p (b c) -> p b c", c=C),
                    in_=qrot.ap().rearrange("p (b c) -> p b c", c=C),
                    transpose=True), 16)
                TRN.inc(sc.dma_start(
                    out=kfm.ap().rearrange("p (b c) -> p b c", c=C),
                    in_=krot.ap().rearrange("p (b c) -> p b c", c=C),
                    transpose=True), 16)
                TRN.inc(sc.dma_start(out=g1fm[:], in_=g1tm[:], transpose=True), 16)
                for srct, dstt in ((qfm, qhm), (kfm, khm)):
                    dv = dstt.ap().rearrange("p (j two c) -> p j two c", two=2, c=C)
                    TRN.inc(sc.dma_start(out=dv[:, :, 0], in_=srct[0:64, :]), 16)
                    TRN.inc(sc.dma_start(out=dv[:, :, 1], in_=srct[64:128, :]), 16)
                if stages < 5:
                    continue
                TRN.wait(pe)
                for grp in range(4):
                    mm = None
                    for hh in range(4):
                        h = grp * 4 + hh
                        mm = pe.matmul(P_et[:, hh * C:(hh + 1) * C],
                                       khm[:, h * C:(h + 1) * C],
                                       qhm[:, h * C:(h + 1) * C],
                                       start=True, stop=True)
                    PE.inc(mm)
                    PE.wait(v)
                    VEC.inc(v.tensor_tensor(
                        et16.ap().rearrange("p (h n) -> p h n", h=H)[
                            :, grp * 4:(grp + 1) * 4],
                        P_et.ap().rearrange("p (h n) -> p h n", h=4),
                        mask16.ap().rearrange("p (one n) -> p one n",
                                              one=1).broadcast_to([128, 4, C]),
                        AL.mult))
                    VEC.wait(pe)
                if gt == 0 and dbg_on:
                    VEC.wait(gp)
                    STR.inc(gp.dma_start(out=dbg_h[4 * 128:5 * 128, :],
                                         in_=et16[:, 0:D]), 16)
                if stages < 6:
                    continue
                mm = None
                for h in range(H):
                    j, b2 = h // 2, 64 * (h % 2)
                    mm = pe.matmul(P_o[:, h * HD:(h + 1) * HD],
                                   et16[:, h * C:(h + 1) * C],
                                   vtm[:, h * HD:(h + 1) * HD],
                                   start=True, stop=(t == 0))
                    if t > 0:
                        mm = pe.matmul(P_o[:, h * HD:(h + 1) * HD],
                                       qhm[:, h * C:(h + 1) * C],
                                       s_16[:, h * HD:(h + 1) * HD],
                                       start=False, stop=True)
                PE.inc(mm)
                PE_O = PE.n
                mm = None
                for nn in range(2):
                    mm = pe.matmul(P_proj[:, nn * 512:(nn + 1) * 512],
                                   g1fm[0:HD, :],
                                   wg2_16[:, nn * 512:(nn + 1) * 512],
                                   start=True, stop=True)
                PE.inc(mm)
                PE.wait(sc)
                SCL.inc(sc.activation(gate16[:], P_proj[:], AF.Sigmoid))
                SCL.wait(pe)
                mm = None
                for h in range(H):
                    mm = pe.matmul(P_s[:, h * HD:(h + 1) * HD],
                                   krot[:, h * HD:(h + 1) * HD],
                                   vtm[:, h * HD:(h + 1) * HD],
                                   start=True, stop=True)
                PE.inc(mm)
                PE.wait(v)
                VEC.inc(v.tensor_add(s_f32[:], s_f32[:], P_s[:]))
                VEC.wait(v)
                VEC.inc(v.tensor_copy(s_16[:], s_f32[:]))
                SCL.wait(v)
                VEC.inc(v.tensor_tensor(og_t[:], P_o[:], gate16[:], AL.mult))
                VEC.wait(v)
                VEC.wait(pe)
                if gt == 0 and dbg_on:
                    VEC.wait(gp)
                    STR.inc(gp.dma_start(out=dbg_h[5 * 128:6 * 128, :], in_=og_t[:]), 16)
                    STR.inc(gp.dma_start(out=dbg_h[7 * 128:8 * 128, :], in_=gate16[:]), 16)
                if stages < 7:
                    continue
                VEC.wait(sc)
                ln_stats_tile(og_t[:])
                VEC.inc(v.tensor_scalar(og_t[:], og_t[:], stat[:, 0:1],
                                        stat[:, 1:2], AL.subtract, AL.mult))
                VEC.wait(v)
                VEC.inc(v.tensor_tensor(og_t[:], og_t[:], gamma_r[:], AL.mult))
                VEC.wait(v)
                VEC.inc(v.tensor_tensor(lnat16[:], og_t[:], beta_r[:], AL.add))
                if gt == 0 and dbg_on:
                    VEC.wait(gp)
                    STR.inc(gp.dma_start(out=dbg_h[6 * 128:7 * 128, :], in_=lnat16[:]), 16)
                VEC.wait(sc)
                LDX.wait(sc)
                STR.wait(sc)
                TRN.inc(sc.dma_start(
                    out=lnafm.ap().rearrange("p (b c) -> p b c", c=C),
                    in_=lnat16.ap().rearrange("p (b c) -> p b c", c=C),
                    transpose=True), 16)
                if stages < 8:
                    continue
                TRN.wait(pe)
                mm = None
                for nn in range(2):
                    for kk in range(8):
                        mm = pe.matmul(
                            P_proj[:, nn * 512:(nn + 1) * 512],
                            lnafm[:, kk * C:(kk + 1) * C],
                            wo16[:, kk * D + nn * 512:kk * D + (nn + 1) * 512],
                            start=(kk == 0), stop=(kk == 7))
                PE.inc(mm)
                PE.wait(v)
                VEC.inc(v.tensor_tensor(r2t[:], P_proj[:], x_ap, AL.add))
                VEC.wait(pe)
                VEC.wait(gp)
                TRN.wait(gp)
                STR.inc(gp.dma_start(out=resid2_h[row0:row0 + 128, :],
                                     in_=r2t[:]), 16)
                if phase == "attn":
                    STR.inc(gp.dma_start(out=out_h[row0:row0 + 128, :],
                                         in_=r2t[:]), 16)
                VEC.wait(v)
                VEC.wait(sc)
                ln_stats_tile(r2t[:])
                VEC.inc(v.tensor_scalar(ln2t[:], r2t[:], stat[:, 0:1],
                                        stat[:, 1:2], AL.subtract, AL.mult))
                VEC.wait(v)
                VEC.inc(v.tensor_reduce(stat[:, 6:7], ln2t[:], AX, AL.max,
                                        apply_absolute_value=True))
                VEC.wait(v)
                if gt == 0:
                    VEC.inc(v.tensor_copy(runmax[:, 0:1], stat[:, 6:7]))
                else:
                    VEC.inc(v.tensor_max(runmax[:, 0:1], runmax[:, 0:1],
                                         stat[:, 6:7]))
                VEC.wait(gp)
                STR.inc(gp.dma_start(out=ln2_h[row0:row0 + 128, :],
                                     in_=ln2t[:]), 16)
                STR.wait(v)
                tile_end_vec.append(VEC.n)

        if phase == "full":
            HT = NSEQ * T // 2
            NOC = FF // 128
            M = MAGIC

            def barrier(bi, cols):
                n = len(cols)
                VEC.wait(gp)
                TRN.wait(gp)
                for j, cc2 in enumerate(cols):
                    STR.inc(gp.dma_start(
                        out=pmax_h[j:j + 1, :].rearrange("one p -> p one"),
                        in_=runmax[:, cc2:cc2 + 1]), 16)
                STR.wait(sy)
                LDX.inc(sy.dma_start(out=smax[0:n, :], in_=pmax_h[0:n, :]), 16)
                LDX.wait(v)
                VEC.inc(v.tensor_reduce(smax[0:n, 0:1], smax[0:n, :], AX, AL.max))
                VEC.wait(gp)
                STR.inc(gp.dma_start(
                    out=cc_in[bi][0:1, 0:n].rearrange("one n -> n one"),
                    in_=smax[0:n, 0:1]), 16)
                STR.wait(gp)
                CCS.inc(gp.collective_compute(
                    "AllReduce", AL.max, replica_groups=[list(range(8))],
                    ins=[cc_in[bi]], outs=[cc_out[bi]]))
                CCS.wait(gp)
                for j in range(n):
                    LDX.inc(gp.dma_start(
                        out=rbc[:, 4 * bi + j:4 * bi + j + 1],
                        in_=cc_out[bi][0:1, j:j + 1].broadcast_to([128, 1])), 16)
                LDX.wait(v)

            def smul(dst, c1, c2):
                VEC.inc(v.tensor_mul(rbc[:, dst:dst + 1], rbc[:, c1:c1 + 1],
                                     rbc[:, c2:c2 + 1]))
                VEC.wait(v)

            def qscale(dst_s, dst_r, src_col, mul):
                VEC.inc(v.tensor_scalar(rbc[:, dst_s:dst_s + 1],
                                        rbc[:, src_col:src_col + 1],
                                        mul, 1e-12, AL.mult, AL.max))
                VEC.wait(v)
                VEC.inc(v.reciprocal(rbc[:, dst_r:dst_r + 1],
                                     rbc[:, dst_s:dst_s + 1]))
                VEC.wait(v)
                VEC.wait(sc)

            barrier(0, [0])
            qscale(1, 2, 0, 1.0 / QMAX_A)

            wdq16 = xln16.ap()[:, 0:1024]
            wtrp = qtm.ap()[:, 0:1024]
            dq3 = slab1.ap()[:, 0:1024].rearrange("p (b k) -> p b k", k=32)
            dq_jobs = []
            for wsrc, wdstT, nch, ncol in ((wup_h, wupT_h, NOC, 1024),
                                           (wgate_h, wgateT_h, NOC, 1024),
                                           (wdown_h, wdownT_h, 8, FF)):
                for ch in range(nch):
                    for cs in range(0, ncol, 1024):
                        dq_jobs.append((wsrc, wdstT, ch, cs))
            dq_vmark = {}
            dq_tmark = {}
            dq_smark = {}
            for ji, (wsrc, wdstT, ch, cs) in enumerate(dq_jobs):
                pp = ji % 2
                wstg = scr[:, 0:1024] if pp == 0 else slab2[:, 0:1024]
                w3 = wstg.rearrange("p (b k) -> p b k", k=32)
                if ji == 0:
                    VEC.wait(sy)
                elif ji >= 2:
                    VEC.wait(sy, dq_vmark[ji - 2])
                TRN.wait(sy)
                LDX.inc(sy.dma_start(
                    out=wstg,
                    in_=wsrc[ch * 128:(ch + 1) * 128, cs:cs + 1024]), 16)
                LDX.wait(v)
                VEC.inc(v.tensor_reduce(wsc[:, 0:32], w3, AX, AL.max,
                                        apply_absolute_value=True))
                VEC.wait(v)
                VEC.inc(v.tensor_scalar(wsc[:, 0:32], wsc[:, 0:32],
                                        1.0 / QMAX_W, 1e-12,
                                        AL.mult, AL.max))
                VEC.wait(v)
                VEC.inc(v.reciprocal(wsc[:, 32:64], wsc[:, 0:32]))
                VEC.wait(v)
                rb3 = wsc[:, 32:64].rearrange(
                    "p (b one) -> p b one", one=1).broadcast_to(
                        [128, 32, 32])
                sb3 = wsc[:, 0:32].rearrange(
                    "p (b one) -> p b one", one=1).broadcast_to(
                        [128, 32, 32])
                VEC.inc(v.tensor_tensor(dq3, w3, rb3, AL.mult))
                dq_vmark[ji] = VEC.n
                VEC.wait(v)
                VEC.inc(v.tensor_scalar(slab1[:, 0:1024],
                                        slab1[:, 0:1024], 1.0, M,
                                        AL.mult, AL.add))
                VEC.wait(v)
                VEC.inc(v.tensor_single_scalar(
                    slab1[:, 0:1024], slab1[:, 0:1024], M, AL.subtract))
                VEC.wait(v)
                if ji >= 1:
                    TRN.wait(v, dq_tmark[ji - 1])
                VEC.inc(v.tensor_tensor(
                    wdq16.rearrange("p (b k) -> p b k", k=32),
                    dq3, sb3, AL.mult))
                VEC.wait(sc)
                LDX.wait(sc)
                STR.wait(sc)
                TRN.inc(sc.dma_start(
                    out=wtrp.rearrange("p (b c) -> p b c", c=128),
                    in_=wdq16.rearrange("p (b c) -> p b c", c=128),
                    transpose=True), 16)
                dq_tmark[ji] = TRN.n
                TRN.wait(gp)
                for kk in range(8):
                    STR.inc(gp.dma_start(
                        out=wdstT[cs + kk * 128:cs + (kk + 1) * 128,
                                  ch * 128:(ch + 1) * 128],
                        in_=wtrp[:, kk * 128:(kk + 1) * 128]), 16)
                dq_smark[ji] = STR.n
            TRN.wait(v)
            STR.wait(v)
            TRN.wait(sy)
            STR.wait(sy)
            TRN.wait(gp)

            VEC.inc(v.memset(runmax[:, 1:3], 0.0))
            VEC.wait(v)
            pe_oc = {}
            for half in range(2):
                for tt2 in range(16):
                    row0 = half * HT + tt2 * 128
                    VEC.wait(sy)
                    TRN.wait(sy)
                    LDX.inc(sy.dma_start(out=ln2t[:],
                                         in_=ln2_h[row0:row0 + 128, :]), 16)
                    LDX.wait(v)
                    VEC.inc(v.tensor_scalar(slab3[:, 0:1024], ln2t[:],
                                            rbc[:, 2:3], M, AL.mult, AL.add))
                    VEC.wait(v)
                    VEC.inc(v.tensor_single_scalar(
                        qrot.ap()[:, 0:1024], slab3[:, 0:1024], M, AL.subtract))
                    if dbg_on and half == 0 and tt2 == 0:
                        VEC.wait(gp)
                        STR.inc(gp.dma_start(out=dbg_h[128:256, :],
                                             in_=qrot.ap()[:, 0:1024]), 16)
                    VEC.wait(sc)
                    LDX.wait(sc)
                    STR.wait(sc)
                    TRN.inc(sc.dma_start(
                        out=ffm.ap().rearrange(
                            "p (b t c) -> p b t c", b=8, t=16)[:, :, tt2],
                        in_=qrot.ap()[:, 0:1024].rearrange(
                            "p (b c) -> p b c", c=128),
                        transpose=True), 16)
                    TRN.wait(v)
                for oc in range(NOC):
                    wb = (oc % 2) * 2048
                    if oc >= 2:
                        PE.wait(sy, pe_oc[(half, oc - 2)])
                    if oc == 0:
                        TRN.wait(sy)
                    LDX.inc(sy.dma_start(
                        out=wtl[:, wb:wb + 1024].rearrange(
                            "p (kk c) -> p kk c", c=128),
                        in_=wupT_h[:, oc * 128:(oc + 1) * 128].rearrange(
                            "(kk p) c -> p kk c", p=128)), 16)
                    LDX.inc(sy.dma_start(
                        out=wtl[:, wb + 1024:wb + 2048].rearrange(
                            "p (kk c) -> p kk c", c=128),
                        in_=wgateT_h[:, oc * 128:(oc + 1) * 128].rearrange(
                            "(kk p) c -> p kk c", p=128)), 16)
                    LDX.wait(pe)
                    if half == 0 and oc == 0:
                        TRN.wait(pe)
                        STR.wait(pe)
                    evac_marks = []
                    for sp in range(4):
                        bk = (sp % 2) * 512
                        if sp >= 2:
                            VEC.wait(pe, evac_marks[sp - 2][0])
                            SCL.wait(pe, evac_marks[sp - 2][1])
                        mmu = None
                        for kk in range(8):
                            mmu = pe.matmul(
                                P_proj[:, bk:bk + 512],
                                wtl[:, wb + kk * 128:wb + (kk + 1) * 128],
                                ffm[:, kk * 2048 + sp * 512:
                                    kk * 2048 + (sp + 1) * 512],
                                start=(kk == 0), stop=(kk == 7))
                        PE.inc(mmu)
                        PE.wait(v)
                        VEC.inc(v.tensor_copy(slab1[:, sp * 512:(sp + 1) * 512],
                                              P_proj[:, bk:bk + 512]))
                        mmg = None
                        for kk in range(8):
                            mmg = pe.matmul(
                                P_o[:, bk:bk + 512],
                                wtl[:, wb + 1024 + kk * 128:
                                    wb + 1024 + (kk + 1) * 128],
                                ffm[:, kk * 2048 + sp * 512:
                                    kk * 2048 + (sp + 1) * 512],
                                start=(kk == 0), stop=(kk == 7))
                        PE.inc(mmg)
                        PE.wait(sc)
                        SCL.inc(sc.copy(slab2[:, sp * 512:(sp + 1) * 512],
                                        P_o[:, bk:bk + 512]))
                        evac_marks.append((VEC.n, SCL.n))
                    VEC.wait(pe)
                    SCL.wait(pe)
                    pe_oc[(half, oc)] = PE.n
                    SCL.wait(v)
                    VEC.inc(v.tensor_reduce(stat[:, 6:7], slab1[:], AX, AL.max,
                                            apply_absolute_value=True))
                    VEC.wait(v)
                    VEC.inc(v.tensor_max(runmax[:, 1:2], runmax[:, 1:2],
                                         stat[:, 6:7]))
                    VEC.inc(v.tensor_reduce(stat[:, 7:8], slab2[:], AX, AL.max,
                                            apply_absolute_value=True))
                    VEC.wait(v)
                    VEC.inc(v.tensor_max(runmax[:, 2:3], runmax[:, 2:3],
                                         stat[:, 7:8]))
                    VEC.wait(gp)
                    SCL.wait(gp)
                    if dbg_on and half == 0 and oc == 0:
                        STR.inc(gp.dma_start(out=dbg_h[256:384, :],
                                             in_=slab1[:, 0:1024]), 16)
                    STR.inc(gp.dma_start(
                        out=upraw_h[oc * 128:(oc + 1) * 128,
                                    half * HT:(half + 1) * HT],
                        in_=slab1[:]), 16)
                    STR.inc(gp.dma_start(
                        out=gateraw_h[oc * 128:(oc + 1) * 128,
                                      half * HT:(half + 1) * HT],
                        in_=slab2[:]), 16)
                    STR.wait(v)
                    STR.wait(sc)

            barrier(1, [1, 2])
            smul(4, 4, 1)
            smul(5, 5, 1)
            qscale(6, 7, 4, 1.0 / QMAX_A)
            qscale(9, 10, 5, 1.0 / QMAX_A)
            smul(7, 7, 1)
            smul(10, 10, 1)
            VEC.inc(v.tensor_scalar_mul(rbc[:, 11:12], rbc[:, 9:10],
                                        1.0 / 65536.0))
            VEC.wait(v)
            VEC.inc(v.memset(runmax[:, 3:4], 0.0))
            VEC.wait(v)
            VEC.wait(sc)

            HH = 1024
            steps = [(h2, oc2, hs) for h2 in range(2) for oc2 in range(NOC)
                     for hs in range(2)]
            for half in range(2):
                for oc in range(NOC):
                    VEC.wait(sy)
                    SCL.wait(sy)
                    LDX.inc(sy.dma_start(
                        out=slab1[:],
                        in_=upraw_h[oc * 128:(oc + 1) * 128,
                                    half * HT:(half + 1) * HT]), 16)
                    LDX.inc(sy.dma_start(
                        out=slab2[:],
                        in_=gateraw_h[oc * 128:(oc + 1) * 128,
                                      half * HT:(half + 1) * HT]), 16)
                    LDX.wait(v)
                    VEC.inc(v.tensor_scalar(slab3[:], slab1[:], rbc[:, 7:8], M,
                                            AL.mult, AL.add))
                    VEC.wait(v)
                    VEC.inc(v.tensor_single_scalar(yqi[:], slab3[:], M,
                                                   AL.subtract))
                    VEC.wait(gp)
                    STR.inc(gp.dma_start(
                        out=yqup_h[oc * 128:(oc + 1) * 128,
                                   half * HT:(half + 1) * HT],
                        in_=yqi[:]), 16)
                    VEC.wait(v)
                    VEC.inc(v.tensor_scalar(slab3[:], slab2[:], rbc[:, 10:11],
                                            M, AL.mult, AL.add))
                    VEC.wait(v)
                    VEC.inc(v.tensor_single_scalar(slab1[:], slab3[:], M,
                                                   AL.subtract))
                    VEC.wait(sc)
                    SCL.inc(sc.activation(slab2[:], slab1[:], AF.Sigmoid,
                                          scale=rbc[:, 9:10]))
                    SCL.wait(v)
                    VEC.inc(v.tensor_scalar(slab3[:], slab2[:], 65536.0, M,
                                            AL.mult, AL.add))
                    VEC.wait(v)
                    VEC.inc(v.tensor_scalar(slab3[:], slab3[:], M, 65535.0,
                                            AL.subtract, AL.min))
                    VEC.wait(v)
                    VEC.inc(v.tensor_mul(slab2[:], slab1[:], slab3[:]))
                    VEC.wait(v)
                    if dbg_on and half == 0 and oc == 0:
                        VEC.wait(gp)
                        STR.inc(gp.dma_start(out=dbg_h[384:512, :],
                                             in_=slab1[:, 0:1024]), 16)
                        STR.inc(gp.dma_start(out=dbg_h[512:640, :],
                                             in_=slab3[:, 0:1024]), 16)
                        STR.inc(gp.dma_start(out=dbg_h[640:768, :],
                                             in_=slab2[:, 0:1024]), 16)
                    VEC.inc(v.tensor_reduce(stat[:, 6:7], slab2[:], AX, AL.max,
                                            apply_absolute_value=True))
                    VEC.wait(v)
                    VEC.inc(v.tensor_max(runmax[:, 3:4], runmax[:, 3:4],
                                         stat[:, 6:7]))
                    VEC.wait(gp)
                    STR.inc(gp.dma_start(
                        out=p_h[oc * 128:(oc + 1) * 128,
                                half * HT:(half + 1) * HT],
                        in_=slab2[:]), 16)
                    STR.wait(v)

            barrier(2, [3])
            smul(13, 11, 8)
            qscale(13, 14, 13, 1.0 / QMAX_A)
            smul(3, 11, 14)

            VEC.inc(v.memset(runmax[:, 0:1], 0.0))
            VEC.wait(v)
            f4_mark = {}
            for si, (half, oc, hs) in enumerate(steps):
                cb = hs * HH
                s1h = slab1.ap()[:, cb:cb + HH]
                s2h = slab2.ap()[:, cb:cb + HH]
                s3h = slab3.ap()[:, cb:cb + HH]
                yqh = yqi[:, cb:cb + HH]
                c0 = half * HT + hs * HH
                if si >= 2:
                    VEC.wait(sy, f4_mark[si - 2][0])
                    STR.wait(sy, f4_mark[si - 2][1])
                LDX.inc(sy.dma_start(
                    out=s1h, in_=p_h[oc * 128:(oc + 1) * 128, c0:c0 + HH]), 16)
                LDX.inc(sy.dma_start(
                    out=yqh, in_=yqup_h[oc * 128:(oc + 1) * 128,
                                        c0:c0 + HH]), 16)
                LDX.wait(v)
                VEC.inc(v.tensor_scalar(s3h, s1h, rbc[:, 3:4], M,
                                        AL.mult, AL.add))
                VEC.wait(v)
                VEC.inc(v.scalar_tensor_tensor(s2h, s3h, M, yqh,
                                               AL.subtract, AL.mult))
                VEC.wait(v)
                VEC.inc(v.tensor_reduce(stat[:, 6:7], s2h, AX, AL.max,
                                        apply_absolute_value=True))
                VEC.wait(v)
                VEC.inc(v.tensor_max(runmax[:, 0:1], runmax[:, 0:1],
                                     stat[:, 6:7]))
                VEC.wait(gp)
                STR.inc(gp.dma_start(
                    out=m2_h[oc * 128:(oc + 1) * 128, c0:c0 + HH],
                    in_=s2h), 16)
                f4_mark[si] = (VEC.n, STR.n)

            barrier(3, [0])
            smul(15, 13, 6)
            smul(2, 15, 12)
            qscale(1, 2, 2, 1.0 / QMAX_A)
            smul(11, 15, 2)

            f5_mark = {}
            for si, (half, oc, hs) in enumerate(steps):
                cb = hs * HH
                s1h = slab1.ap()[:, cb:cb + HH]
                s3h = slab3.ap()[:, cb:cb + HH]
                o16h = o16s.ap()[:, cb:cb + HH]
                c0 = half * HT + hs * HH
                if si >= 2:
                    VEC.wait(sy, f5_mark[si - 2][0])
                    STR.wait(sy, f5_mark[si - 2][1])
                LDX.inc(sy.dma_start(
                    out=s1h, in_=m2_h[oc * 128:(oc + 1) * 128,
                                      c0:c0 + HH]), 16)
                LDX.wait(v)
                VEC.inc(v.tensor_scalar(s3h, s1h, rbc[:, 11:12], M,
                                        AL.mult, AL.add))
                VEC.wait(v)
                VEC.inc(v.tensor_single_scalar(o16h, s3h, M, AL.subtract))
                VEC.wait(gp)
                STR.inc(gp.dma_start(
                    out=o_h[oc * 128:(oc + 1) * 128, c0:c0 + HH],
                    in_=o16h), 16)
                f5_mark[si] = (VEC.n, STR.n)
            PE.wait(sy)
            wd_sb = [wq16, wk16, wv16]
            for kk in range(NOC):
                LDX.inc(sy.dma_start(
                    out=wd_sb[kk // 8][:, (kk % 8) * 1024:(kk % 8 + 1) * 1024],
                    in_=wdownT_h[kk * 128:(kk + 1) * 128, :]), 16)
            STR.wait(sy)
            tile_pe = {}
            for gt2 in range(NTILES):
                fb = (gt2 % 2) * NOC * 128
                if gt2 >= 2:
                    PE.wait(sy, tile_pe[gt2 - 2])
                LDX.inc(sy.dma_start(
                    out=ffm.ap()[:, fb:fb + NOC * 128].rearrange(
                        "p (kk c) -> p kk c", c=128),
                    in_=o_h[:, gt2 * 128:(gt2 + 1) * 128].rearrange(
                        "(kk p) c -> p kk c", p=128)), 16)
                LDX.wait(pe)
                mm = None
                for nn in range(2):
                    for kk in range(NOC):
                        wdt = wd_sb[kk // 8][:, (kk % 8) * 1024 + nn * 512:
                                             (kk % 8) * 1024 + (nn + 1) * 512]
                        mm = pe.matmul(
                            P_proj[:, nn * 512:(nn + 1) * 512],
                            ffm[:, fb + kk * 128:fb + (kk + 1) * 128],
                            wdt, start=(kk == 0), stop=(kk == NOC - 1))
                PE.inc(mm)
                tile_pe[gt2] = PE.n
                row0 = gt2 * 128
                VEC.wait(sy)
                LDX.inc(sy.dma_start(out=r2t[:],
                                     in_=resid2_h[row0:row0 + 128, :]), 16)
                LDX.wait(v)
                PE.wait(v)
                VEC.inc(v.scalar_tensor_tensor(ln2t[:], P_proj[:],
                                               rbc[:, 1:2], r2t[:],
                                               AL.mult, AL.add))
                VEC.wait(pe)
                VEC.wait(gp)
                STR.inc(gp.dma_start(out=out_h[row0:row0 + 128, :],
                                     in_=ln2t[:]), 16)
                STR.wait(v)

        STR.wait(gp)

        block.vector(lambda e: v.replay(e))
        block.scalar(lambda e: sc.replay(e))
        block.tensor(lambda e: pe.replay(e))
        block.gpsimd(lambda e: gp.replay(e))
        block.sync(lambda e: sy.replay(e))
    return nc


import numpy as np
from concourse.bass_utils import run_bass_kernel_spmd

_NC_CACHE = {}


def kernel(L0_attn_x, L0_attn_Wq, L0_attn_Wk, L0_attn_Wv, L0_attn_Wo,
           L0_attn_Wg1, L0_attn_Wg2, L0_attn_gamma, L0_attn_beta,
           L0_ffn_w_gate, L0_ffn_w_up, L0_ffn_w_down):
    if "nc" not in _NC_CACHE:
        _NC_CACHE["nc"] = build("full")
    nc = _NC_CACHE["nc"]
    x = np.asarray(L0_attn_x, dtype=np.float32)
    common = dict(
        Wq=np.ascontiguousarray(L0_attn_Wq, dtype=np.float32),
        Wk=np.ascontiguousarray(L0_attn_Wk, dtype=np.float32),
        Wv=np.ascontiguousarray(L0_attn_Wv, dtype=np.float32),
        Wo=np.ascontiguousarray(L0_attn_Wo, dtype=np.float32),
        Wg1=np.ascontiguousarray(L0_attn_Wg1, dtype=np.float32),
        Wg2=np.ascontiguousarray(L0_attn_Wg2, dtype=np.float32),
        gamma=np.ascontiguousarray(np.asarray(L0_attn_gamma, dtype=np.float32)
                                   .reshape(1, D)),
        beta=np.ascontiguousarray(np.asarray(L0_attn_beta, dtype=np.float32)
                                  .reshape(1, D)),
        w_gate=np.ascontiguousarray(L0_ffn_w_gate, dtype=np.float32),
        w_up=np.ascontiguousarray(L0_ffn_w_up, dtype=np.float32),
        w_down=np.ascontiguousarray(L0_ffn_w_down, dtype=np.float32),
    )
    in_maps = []
    for c in range(8):
        shard = np.ascontiguousarray(
            x[NSEQ * c:NSEQ * (c + 1)].reshape(NSEQ * T, D))
        m = dict(common)
        m["x"] = shard
        in_maps.append(m)
    res = run_bass_kernel_spmd(nc, in_maps, core_ids=list(range(8)))
    out = np.concatenate(
        [np.asarray(res.results[c]["out"]).reshape(NSEQ, T, D)
         for c in range(8)], axis=0)
    return out
